# version 02 (final)
# baseline (speedup 1.0000x reference)
import numpy as np

import concourse.bass as bass
import concourse.mybir as mybir
from concourse.bass_utils import run_bass_kernel_spmd
from concourse.alu_op_type import AluOpType

F32 = mybir.dt.float32
BF16 = mybir.dt.bfloat16
F16 = mybir.dt.float16
AF = mybir.ActivationFunctionType
ADD, SUB, MUL = AluOpType.add, AluOpType.subtract, AluOpType.mult

B, T, H, FF, E, NH, HD = 8, 2048, 1024, 4096, 8, 16, 64
HJ = H // 128
FJ = FF // 128
NSC = T // 512
NST = T // 128

P_OFF, _acc = [0] * NST, 0
for _si in range(NST):
    P_OFF[_si] = _acc
    _acc += T - 128 * _si
P_TOT = _acc


class Ctr:
    __slots__ = ("h", "n", "step")

    def __init__(self, h, step):
        self.h, self.n, self.step = h, 0, step


def mk(fn, *a, **kw):
    return lambda eng, fn=fn, a=a, kw=kw: fn(*a, **kw)


def chunks512(c0, c1):
    out, c = [], c0
    while c < c1:
        n = min(512 - (c % 512), c1 - c)
        out.append((c, n))
        c += n
    return out


def build_program():
    nc = bass.Bass("TRN2", target_bir_lowering=False, debug=False)

    d = {}
    def din(name, shape, dt=F32):
        d[name] = nc.dram_tensor(name, shape, dt, kind="ExternalInput").ap()
    din("xT", [H, T])
    din("wqkv", [H, 3 * H])
    din("wproj", [H, H])
    din("gw1t", [H, H])
    din("gw2t", [H, E])
    din("gb1c", [128, 8])
    din("gb2r", [1, E])
    din("gb2c", [E, 1])
    din("w1t", [H, FF])
    din("w2t", [FF, H])
    din("tw1t", [E, H, FF])
    din("tw2t", [E, FF, H])
    din("b1c", [128, 32])
    din("b2r", [1, H])
    din("tb1", [E, FF])
    din("tb2", [E, H])
    din("cwb", [128, NH - 1])
    din("cosT", [NH * 32, T])
    din("sinT", [NH * 32, T])
    din("trimask", [128, 128])
    outT = nc.dram_tensor("outT", [H, T], F32, kind="ExternalOutput").ap()
    xres = nc.dram_tensor("xres", [H, T], F32, kind="Internal").ap()
    u_scr = nc.dram_tensor("u_scr", [FF, T], BF16, kind="Internal").ap()

    Q = {k: [] for k in ("pe", "act", "dve", "sp", "gp")}
    WAITED = {k: {} for k in Q}

    ARENA_COLS = 102400
    with (
        nc.sbuf_tensor("arena", [128, ARENA_COLS], BF16) as AR,
        nc.psum_tensor("psA", [128, 2048], F32) as psA,
        nc.psum_tensor("psB", [128, 2048], F32) as psB,
        nc.semaphore("cP") as hP,
        nc.semaphore("cA") as hA,
        nc.semaphore("cV") as hV,
        nc.semaphore("cX") as hX,
        nc.semaphore("cW") as hW,
        nc.Block() as blk,
    ):
        cP, cA, cV = Ctr(hP, 1), Ctr(hA, 1), Ctr(hV, 1)
        cX, cW = Ctr(hX, 16), Ctr(hW, 16)

        def emit(qn, fn, waits=(), inc=None):
            wl = []
            for c, v in waits:
                if v <= 0:
                    continue
                k = id(c)
                if WAITED[qn].get(k, -1) >= v:
                    continue
                WAITED[qn][k] = v
                wl.append((c, v))
            def run(eng, wl=tuple(wl), fn=fn, inc=inc):
                for c, v in wl:
                    eng.wait_ge(c.h, v)
                r = fn(eng)
                if inc is not None:
                    r.then_inc(inc.h, inc.step)
            Q[qn].append(run)
            if inc is not None:
                cnt = inc
                cnt.n += cnt.step
                return cnt.n
            return None

        NOPS = {"pe": nc.tensor, "act": nc.scalar, "dve": nc.vector,
                "sp": nc.sync, "gp": nc.gpsimd}

        def barrier():
            snap = [(cP, cP.n), (cA, cA.n), (cV, cV.n), (cX, cX.n), (cW, cW.n)]
            for qn, eng in NOPS.items():
                emit(qn, mk(eng.nop), waits=snap)

        def V(off, parts, cols, dt):
            mul = 2 if dt == F32 else 1
            n = cols * mul
            assert off + n <= ARENA_COLS, (off, n)
            ap = AR[:, off:off + n]
            if dt != BF16:
                ap = ap.bitcast(dt)
            if parts != 128:
                ap = ap[0:parts, :]
            return ap

        o = 0
        def alloc(parts, cols, dt):
            nonlocal o
            a = V(o, parts, cols, dt)
            o += cols * (2 if dt == F32 else 1)
            o = (o + 31) & ~31
            return a
        trimask = alloc(128, 128, F32)
        cwb = alloc(128, NH - 1, F32)
        gb1c = alloc(128, 8, F32)
        b1c = alloc(128, 32, F32)
        g_bcast = alloc(128, 8, F32)
        mb1 = alloc(128, 32, F32)
        hacc = alloc(128, 32, F32)
        hbar = alloc(128, 8, BF16)
        bias_cols = alloc(128, NST, F32)
        casc_raw = alloc(128, NST, F32)
        gw2t_sb = alloc(128, 64, BF16)
        ones_col = alloc(128, 1, BF16)
        hscr = [alloc(128, 512, BF16), alloc(128, 512, BF16)]
        a_bfr = alloc(1, 512, BF16)
        c_bfr = alloc(1, 512, BF16)
        ones_r128b = alloc(1, 128, BF16)
        ones_r512b = alloc(1, 512, BF16)
        onesf128 = alloc(1, 128, F32)
        b2r_bf = alloc(1, H, BF16)
        mb2row = alloc(1, H, BF16)
        grow_f = alloc(1, E, F32)
        gb2r_sb = alloc(1, E, F32)
        gcol_f = alloc(E, 1, F32)
        gcol_h = alloc(E, 1, F16)
        gb2c_sb = alloc(E, 1, F32)
        PB = (o + 63) & ~63

        XN = PB
        TWC0, TWC1 = PB + 16384, PB + 24576
        SSCR, MW1C, MW2T = PB + 32768, PB + 40960, PB + 49152
        ACC2 = PB + 81920
        UBLK0, UBLK1 = PB + 90112, PB + 90624
        AB_, CB_ = PB + 91136, PB + 91648
        XBCH, XSCH = TWC0, TWC0 + 4096
        US, XNEW, XOLD, XC = TWC0, SSCR, MW1C, MW1C
        P_SB = PB
        QK_SB, V_SB, O_ALL = PB + 17408, PB + 50176, PB + 66816
        TRIG, MZ, MZ2 = PB + 83200, PB + 87296, PB + 89344
        ROTQ, ROTK = PB + 91392, PB + 93440
        WV_, WQP0, WQP1 = O_ALL, O_ALL + 8192, O_ALL + 9216
        WPROJ, AXOLD, AXNEW = P_SB, QK_SB, QK_SB + 8192
        RF32, RBF, RB_ = ROTQ, TRIG, TRIG + 2048
        assert PB + 95488 <= ARENA_COLS, PB

        xn = V(XN, 128, 16384, BF16)
        lnrows = V(MZ, 1, 2048, F32)

        ld_last = 0
        for name, dst in (("trimask", trimask), ("cwb", cwb), ("gb1c", gb1c),
                          ("b1c", b1c), ("gb2r", gb2r_sb), ("gb2c", gb2c_sb)):
            ld_last = emit("sp", mk(nc.sync.dma_start, out=dst, in_=d[name]), inc=cX)
        CONST_W = emit("gp", mk(nc.gpsimd.dma_start, out=b2r_bf, in_=d["b2r"]),
                       inc=cW)
        CONST_W = emit("gp", mk(nc.gpsimd.dma_start,
                                out=gw2t_sb.rearrange("p (j e) -> p j e", e=8),
                                in_=d["gw2t"].rearrange("(j p) e -> p j e", p=128)),
                       inc=cW)
        for t_ in (ones_col, ones_r128b, ones_r512b, onesf128):
            CONST_V = emit("dve", mk(nc.vector.memset, t_, 1.0), inc=cV)

        def emit_ln(src):
            xc = V(XC, 128, 4096, F32)
            xb = V(XBCH, 128, 4096, BF16)
            xs = V(XSCH, 128, 4096, BF16)
            ab = V(AB_, 128, 512, BF16)
            cb = V(CB_, 128, 512, BF16)
            r0, r1 = lnrows[:, 0:512], lnrows[:, 512:1024]
            r2, r3 = lnrows[:, 1024:1536], lnrows[:, 1536:2048]
            src_r = src.rearrange("(j p) s -> p j s", p=128)
            last_norm = 0
            for sc in range(NSC):
                s0 = 512 * sc
                vld = emit("sp", mk(nc.sync.dma_start,
                                    out=xc.rearrange("p (j s) -> p j s", s=512),
                                    in_=src_r[:, :, s0:s0 + 512]),
                           waits=[(cV, last_norm)], inc=cX)
                vcast = emit("dve", mk(nc.vector.tensor_copy, xb, xc),
                             waits=[(cX, vld)], inc=cV)
                vsq = emit("dve", mk(nc.vector.tensor_tensor, xs, xb, xb, MUL),
                           inc=cV)
                for j in range(HJ):
                    emit("pe", mk(nc.tensor.matmul, psA[0:1, 0:512], ones_col,
                                  xb[:, 512 * j:512 * (j + 1)],
                                  start=(j == 0), stop=(j == HJ - 1)),
                         waits=[(cV, vsq), (cV, CONST_V)] if j == 0 else ())
                pst = 0
                for j in range(HJ):
                    pst = emit("pe", mk(nc.tensor.matmul, psA[0:1, 512:1024],
                                        ones_col, xs[:, 512 * j:512 * (j + 1)],
                                        start=(j == 0), stop=(j == HJ - 1)),
                               inc=cP if j == HJ - 1 else None)
                emit("dve", mk(nc.vector.tensor_scalar, r0, psA[0:1, 0:512],
                               1.0 / H, None, MUL), waits=[(cP, pst)], inc=cV)
                emit("dve", mk(nc.vector.tensor_scalar, r1, psA[0:1, 512:1024],
                               1.0 / H, None, MUL), inc=cV)
                emit("dve", mk(nc.vector.tensor_tensor, r2, r0, r0, MUL), inc=cV)
                emit("dve", mk(nc.vector.tensor_tensor, r3, r1, r2, SUB), inc=cV)
                vv = emit("dve", mk(nc.vector.tensor_scalar, r1, r3, 1e-5,
                                    None, ADD), inc=cV)
                emit("act", mk(nc.scalar.activation, r2, r1, AF.Ln),
                     waits=[(cV, vv)], inc=cA)
                va = emit("act", mk(nc.scalar.activation, r3, r2, AF.Exp,
                                    scale=-0.5), inc=cA)
                emit("dve", mk(nc.vector.tensor_tensor, r1, r0, r3, MUL),
                     waits=[(cA, va)], inc=cV)
                emit("dve", mk(nc.vector.tensor_scalar, c_bfr, r1, -1.0, None, MUL),
                     inc=cV)
                vrw = emit("dve", mk(nc.vector.tensor_copy, a_bfr, r3), inc=cV)
                emit("pe", mk(nc.tensor.matmul, psA[:, 1024:1536], ones_r128b,
                              a_bfr, start=True, stop=True), waits=[(cV, vrw)])
                po = emit("pe", mk(nc.tensor.matmul, psA[:, 1536:2048], ones_r128b,
                                   c_bfr, start=True, stop=True), inc=cP)
                emit("act", mk(nc.scalar.activation, ab, psA[:, 1024:1536], AF.Copy),
                     waits=[(cP, po)], inc=cA)
                va2 = emit("act", mk(nc.scalar.activation, cb, psA[:, 1536:2048],
                                     AF.Copy), inc=cA)
                for j in range(HJ):
                    xnb = xn[:, 2048 * j + s0:2048 * j + s0 + 512]
                    xsb = xs[:, 512 * j:512 * (j + 1)]
                    emit("dve", mk(nc.vector.tensor_tensor, xsb,
                                   xb[:, 512 * j:512 * (j + 1)], ab, MUL),
                         waits=[(cA, va2)], inc=cV)
                    last_norm = emit("dve", mk(nc.vector.tensor_tensor, xnb, xsb,
                                               cb, ADD), inc=cV)
            return last_norm

        def emit_ffn(src, dst):
            barrier()
            vln = emit_ln(src)
            ln_pe = cP.n
            tb1_sb = V(ACC2, E, FF, F16)
            tb2_sb = V(ACC2 + 4096, E, H, F16)
            tbl = emit("gp", mk(nc.gpsimd.dma_start, out=tb1_sb, in_=d["tb1"]),
                       inc=cW)
            tbl = emit("gp", mk(nc.gpsimd.dma_start, out=tb2_sb, in_=d["tb2"]),
                       inc=cW)
            gw1t_sb = V(GW1T_, 128, 8192, BF16)
            vg = emit("gp", mk(nc.gpsimd.dma_start,
                               out=gw1t_sb.rearrange("p (j c) -> p j c", c=1024),
                               in_=d["gw1t"].rearrange("(j p) c -> p j c", p=128)),
                      inc=cW)
            relu_v = {}
            gk = 0
            for ot in range(HJ):
                for sc in range(NSC):
                    buf = gk % 2
                    ps = psB[:, 512 * buf:512 * buf + 512]
                    pm = 0
                    for j in range(HJ):
                        w = ([(cV, vln), (cW, vg), (cA, relu_v.get(gk - 2, 0))]
                             if j == 0 else ())
                        pm = emit("pe", mk(nc.tensor.matmul, ps,
                            gw1t_sb[:, 1024 * j + 128 * ot:1024 * j + 128 * ot + 128],
                            xn[:, 2048 * j + 512 * sc:2048 * j + 512 * sc + 512],
                            start=(j == 0), stop=(j == HJ - 1)),
                            waits=w, inc=cP if j == HJ - 1 else None)
                    col = sc * 8 + ot
                    relu_v[gk] = emit("act", mk(nc.scalar.activation, hscr[buf], ps,
                                                AF.Relu, bias=gb1c[:, ot:ot + 1],
                                                scale=1.0,
                                                accum_out=hacc[:, col:col + 1]),
                                      waits=[(cP, pm), (cX, ld_last)], inc=cA)
                    gk += 1
            relu_all = relu_v[gk - 1]
            emit("dve", mk(nc.vector.tensor_tensor, casc_raw[:, 0:8],
                           hacc[:, 0:8], hacc[:, 8:16], ADD),
                 waits=[(cA, relu_all)], inc=cV)
            emit("dve", mk(nc.vector.tensor_tensor, casc_raw[:, 8:16],
                           hacc[:, 16:24], hacc[:, 24:32], ADD), inc=cV)
            emit("dve", mk(nc.vector.drain))
            emit("dve", mk(nc.vector.tensor_tensor, bias_cols[:, 0:8],
                           casc_raw[:, 0:8], casc_raw[:, 8:16], ADD), inc=cV)
            emit("dve", mk(nc.vector.drain))
            vhb = emit("dve", mk(nc.vector.tensor_scalar, hbar, bias_cols[:, 0:8],
                                 1.0 / T, None, MUL), inc=cV)
            pg = pg2 = 0
            for ot in range(HJ):
                pg = emit("pe", mk(nc.tensor.matmul, psB[0:1, 1024:1032],
                                   hbar[:, ot:ot + 1], gw2t_sb[:, 8 * ot:8 * ot + 8],
                                   start=(ot == 0), stop=(ot == HJ - 1)),
                          waits=[(cV, vhb), (cW, CONST_W)] if ot == 0 else (),
                          inc=cP if ot == HJ - 1 else None)
            for ot in range(HJ):
                pg2 = emit("pe", mk(nc.tensor.matmul, psB[0:8, 1536:1537],
                                    gw2t_sb[:, 8 * ot:8 * ot + 8], hbar[:, ot:ot + 1],
                                    start=(ot == 0), stop=(ot == HJ - 1)),
                           inc=cP if ot == HJ - 1 else None)
            emit("dve", mk(nc.vector.tensor_tensor, grow_f, psB[0:1, 1024:1032],
                           gb2r_sb, ADD), waits=[(cP, pg2), (cX, ld_last)], inc=cV)
            emit("dve", mk(nc.vector.tensor_tensor, gcol_f, psB[0:8, 1536:1537],
                           gb2c_sb, ADD), inc=cV)
            emit("dve", mk(nc.vector.drain))
            vgc = emit("dve", mk(nc.vector.tensor_copy, gcol_h, gcol_f), inc=cV)
            pgb = emit("pe", mk(nc.tensor.matmul, psA[:, 0:8], onesf128, grow_f,
                                start=True, stop=True), waits=[(cV, vgc)], inc=cP)
            vgb = emit("dve", mk(nc.vector.tensor_copy, g_bcast, psA[:, 0:8]),
                       waits=[(cP, pgb)], inc=cV)
            pmb = 0
            for ft in range(32):
                pmb = emit("pe", mk(nc.tensor.matmul, psA[:, 512 + ft:513 + ft],
                                    tb1_sb[:, 128 * ft:128 * ft + 128], gcol_h,
                                    start=True, stop=True),
                           waits=[(cV, vgc), (cW, tbl)] if ft == 0 else (),
                           inc=cP if ft == 31 else None)
            vmb1 = emit("dve", mk(nc.vector.tensor_tensor, mb1, psA[:, 512:544],
                                  b1c, ADD), waits=[(cP, pmb)], inc=cV)
            emit("pe", mk(nc.tensor.matmul, psA[0:1, 1024:1536], gcol_h,
                          tb2_sb[:, 0:512], start=True, stop=True))
            pm2 = emit("pe", mk(nc.tensor.matmul, psA[0:1, 1536:2048], gcol_h,
                                tb2_sb[:, 512:1024], start=True, stop=True), inc=cP)
            emit("dve", mk(nc.vector.tensor_tensor, mb2row[:, 0:512],
                           psA[0:1, 1024:1536], b2r_bf[:, 0:512], ADD),
                 waits=[(cP, pm2), (cW, CONST_W)], inc=cV)
            vmb2 = emit("dve", mk(nc.vector.tensor_tensor, mb2row[:, 512:1024],
                                  psA[0:1, 1536:2048], b2r_bf[:, 512:1024], ADD),
                        inc=cV)
            gate_pe = cP.n

            twc = [V(TWC0, 128, 8192, F16), V(TWC1, 128, 8192, F16)]
            sscr = V(SSCR, 128, 8192, F16)
            mw1c = V(MW1C, 128, 8192, F16)
            mw2t = V(MW2T, 128, 32768, F16)
            ublk = [V(UBLK0, 128, 512, BF16), V(UBLK1, 128, 512, BF16)]
            tw1r = d["tw1t"].rearrange("e (j p) f -> e p j f", p=128)
            w1r = d["w1t"].rearrange("(j p) f -> p j f", p=128)
            acc2 = V(ACC2, 128, 8192, F16)
            state = {"li": 0, "cons": [0, 0]}
            gelu_vals, udma = {}, {}

            def comb(dst_ap, srcs, pre_w):
                vdone = 0
                acc = dst_ap
                for e in range(9):
                    buf = state["li"] % 2
                    lv = emit("gp", srcs[e](twc[buf]),
                              waits=[(cV, state["cons"][buf])] + (pre_w if e < 2 else []),
                              inc=cW)
                    if e == 0:
                        vdone = emit("dve", mk(nc.vector.tensor_scalar, dst_ap,
                                               twc[buf], g_bcast[:, 0:1], None, MUL),
                                     waits=[(cW, lv), (cV, vgb)] + pre_w, inc=cV)
                        acc = dst_ap
                    elif e < 8:
                        emit("dve", mk(nc.vector.tensor_scalar, sscr, twc[buf],
                                       g_bcast[:, e:e + 1], None, MUL),
                             waits=[(cW, lv)], inc=cV)
                        nxt = acc2 if acc is dst_ap else dst_ap
                        vdone = emit("dve", mk(nc.vector.tensor_tensor, nxt,
                                               acc, sscr, ADD), inc=cV)
                        acc = nxt
                    else:
                        assert acc is acc2
                        vdone = emit("dve", mk(nc.vector.tensor_tensor, dst_ap,
                                               acc, twc[buf], ADD),
                                     waits=[(cW, lv)], inc=cV)
                    state["cons"][buf] = vdone
                    state["li"] += 1
                return vdone

            mm1_last = 0
            for fc in range(4):
                srcs1 = [(lambda dstb, e=e, fc=fc: mk(nc.gpsimd.dma_start,
                            out=dstb.rearrange("p (j f) -> p j f", f=1024),
                            in_=tw1r[e, :, :, 1024 * fc:1024 * fc + 1024]))
                         for e in range(E)]
                srcs1.append(lambda dstb, fc=fc: mk(nc.gpsimd.dma_start,
                            out=dstb.rearrange("p (j f) -> p j f", f=1024),
                            in_=w1r[:, :, 1024 * fc:1024 * fc + 1024]))
                v1done = comb(mw1c, srcs1,
                              [(cP, max(mm1_last, ln_pe, gate_pe)), (cV, vln)])
                for ft in range(8):
                    gft = 8 * fc + ft
                    for sc in range(NSC):
                        k = gft * NSC + sc
                        buf = k % 2
                        ps = psB[:, 512 * buf:512 * buf + 512]
                        pm = 0
                        for j in range(HJ):
                            w = ()
                            if j == 0:
                                w = [(cV, v1done), (cA, gelu_vals.get(k - 2, 0)),
                                     (cV, vmb1)]
                            pm = emit("pe", mk(nc.tensor.matmul, ps,
                                mw1c[:, 1024 * j + 128 * ft:1024 * j + 128 * ft + 128],
                                xn[:, 2048 * j + 512 * sc:2048 * j + 512 * sc + 512],
                                start=(j == 0), stop=(j == HJ - 1)),
                                waits=w, inc=cP if j == HJ - 1 else None)
                        gw = [(cP, pm)]
                        if k - 2 in udma:
                            gw.append((cX, udma[k - 2]))
                        gelu_vals[k] = emit("act", mk(nc.scalar.activation,
                            ublk[k % 2], ps,
                            AF.Gelu_apprx_tanh, bias=mb1[:, gft:gft + 1], scale=1.0),
                            waits=gw, inc=cA)
                        udma[k] = emit("sp", mk(nc.sync.dma_start,
                            out=u_scr[128 * gft:128 * gft + 128,
                                      512 * sc:512 * sc + 512],
                            in_=ublk[k % 2]),
                            waits=[(cA, gelu_vals[k])], inc=cX)
                mm1_last = cP.n
                srcs2 = [(lambda dstb, e=e, fc=fc: mk(nc.gpsimd.dma_start,
                            out=dstb.rearrange("p (j h) -> p j h", h=1024),
                            in_=d["tw2t"][e, 1024 * fc:1024 * fc + 1024, :].rearrange(
                                "(j p) h -> p j h", p=128)))
                         for e in range(E)]
                srcs2.append(lambda dstb, fc=fc: mk(nc.gpsimd.dma_start,
                            out=dstb.rearrange("p (j h) -> p j h", h=1024),
                            in_=d["w2t"][1024 * fc:1024 * fc + 1024, :].rearrange(
                                "(j p) h -> p j h", p=128)))
                comb(mw2t[:, 8192 * fc:8192 * (fc + 1)], srcs2, [])

            barrier()
            us = V(US, 128, 16384, BF16)
            xold = V(XOLD, 128, 4096, F32)
            xnew = V(XNEW, 128, 4096, F32)
            src_r = src.rearrange("(j p) s -> p j s", p=128)
            dst_r = dst.rearrange("(j p) s -> p j s", p=128)
            u_r = u_scr.rearrange("(j p) s -> p j s", p=128)
            v2done = cV.n
            last_udma = udma[FJ * NSC - 1]
            stt_vals = {}
            xnew_dma, last_stt = 0, 0
            for sc in range(NSC):
                s0 = 512 * sc
                uld = emit("sp", mk(nc.sync.dma_start,
                                    out=us.rearrange("p (j s) -> p j s", s=512),
                                    in_=u_r[:, :, s0:s0 + 512]),
                           waits=[(cX, last_udma), (cV, last_stt)], inc=cX)
                xld = emit("sp", mk(nc.sync.dma_start,
                                    out=xold.rearrange("p (j s) -> p j s", s=512),
                                    in_=src_r[:, :, s0:s0 + 512]),
                           waits=[(cV, last_stt)], inc=cX)
                for ht in range(HJ):
                    k = sc * HJ + ht
                    buf = k % 2
                    ps = psB[:, 1024 + 512 * buf:1024 + 512 * buf + 512]
                    for fj in range(FJ):
                        w = ()
                        if fj == 0:
                            w = [(cX, uld), (cV, v2done),
                                 (cV, stt_vals.get(k - 2, 0))]
                        emit("pe", mk(nc.tensor.matmul, ps,
                            mw2t[:, 1024 * fj + 128 * ht:1024 * fj + 128 * ht + 128],
                            us[:, 512 * fj:512 * fj + 512],
                            start=(fj == 0), stop=False), waits=w)
                    pm = emit("pe", mk(nc.tensor.matmul, ps,
                                       mb2row[0:1, 128 * ht:128 * ht + 128],
                                       ones_r512b, start=False, stop=True),
                              waits=[(cV, vmb2)], inc=cP)
                    stt_vals[k] = emit("dve", mk(nc.vector.scalar_tensor_tensor,
                        xnew[:, 512 * ht:512 * ht + 512], ps, 0.5,
                        xold[:, 512 * ht:512 * ht + 512], MUL, ADD),
                        waits=[(cP, pm), (cX, xld), (cX, xnew_dma)], inc=cV)
                last_stt = stt_vals[sc * HJ + HJ - 1]
                xnew_dma = emit("sp", mk(nc.sync.dma_start,
                                         out=dst_r[:, :, s0:s0 + 512],
                                         in_=xnew.rearrange("p (j s) -> p j s", s=512)),
                                waits=[(cV, last_stt)], inc=cX)
            return xnew_dma

        GW1T_ = MW2T

        def emit_attn(src, dst):
            barrier()
            vln = emit_ln(src)
            qk = V(QK_SB, 128, 32768, BF16)
            vsb = V(V_SB, 128, 16640, BF16)
            wv = V(WV_, 128, 8192, BF16)
            wqp = [V(WQP0, 128, 1024, BF16), V(WQP1, 128, 1024, BF16)]
            p_sb = V(P_SB, 128, P_TOT, BF16)
            o_all = V(O_ALL, 128, 16384, BF16)
            wqkv_r = d["wqkv"].rearrange("(j p) c -> p j c", p=128)
            vones = emit("dve", mk(nc.vector.memset,
                                   vsb.rearrange("p (a c) -> p a c", c=65)[:, :, 64:65],
                                   1.0), inc=cV)
            wld, pe_ct, cpv = {}, {}, {}
            for ct in range(16):
                wld[ct] = emit("gp", mk(nc.gpsimd.dma_start,
                                        out=wqp[ct % 2].rearrange("p (j c) -> p j c", c=128),
                                        in_=wqkv_r[:, :, 128 * ct:128 * ct + 128]),
                               waits=[(cP, pe_ct.get(ct - 2, 0))], inc=cW)
                for sc in range(NSC):
                    k = ct * NSC + sc
                    ps = psA[:, 512 * (k % 2):512 * (k % 2) + 512]
                    pm = 0
                    for j in range(HJ):
                        w = ()
                        if j == 0:
                            w = [(cV, vln), (cW, wld[ct]), (cV, cpv.get(k - 2, 0))]
                        pm = emit("pe", mk(nc.tensor.matmul, ps,
                            wqp[ct % 2][:, 128 * j:128 * j + 128],
                            xn[:, 2048 * j + 512 * sc:2048 * j + 512 * sc + 512],
                            start=(j == 0), stop=(j == HJ - 1)),
                            waits=w, inc=cP if j == HJ - 1 else None)
                    pe_ct[ct] = pm
                    is_q = ct < 8
                    hp = 2 * (ct % 8)
                    pr = 0 if is_q else 64
                    emit("dve", mk(nc.vector.tensor_copy,
                        qk[pr:pr + 64, 2048 * hp + 512 * sc:2048 * hp + 512 * sc + 512],
                        ps[0:64, :]), waits=[(cP, pm)], inc=cV)
                    cpv[k] = emit("dve", mk(nc.vector.tensor_copy,
                        qk[pr:pr + 64, 2048 * (hp + 1) + 512 * sc:2048 * (hp + 1) + 512 * sc + 512],
                        ps[64:128, :]), inc=cV)
            qk_done = cV.n
            wvld = emit("gp", mk(nc.gpsimd.dma_start,
                                 out=wv.rearrange("p (j c) -> p j c", c=1024),
                                 in_=wqkv_r[:, :, 2048:3072]),
                        waits=[(cV, qk_done)], inc=cW)
            vcp = {}
            for st in range(16):
                for cv2 in range(2):
                    k = st * 2 + cv2
                    ps = psA[:, 1024 + 512 * (k % 2):1024 + 512 * (k % 2) + 512]
                    pm = 0
                    for j in range(HJ):
                        w = ()
                        if j == 0:
                            w = [(cW, wvld), (cV, vcp.get(k - 2, 0))]
                        pm = emit("pe", mk(nc.tensor.matmul, ps,
                            xn[:, 2048 * j + 128 * st:2048 * j + 128 * st + 128],
                            wv[:, 1024 * j + 512 * cv2:1024 * j + 512 * cv2 + 512],
                            start=(j == 0), stop=(j == HJ - 1)),
                            waits=w, inc=cP if j == HJ - 1 else None)
                    dstv = vsb.rearrange("p (st h c) -> p st h c", st=16, h=16)[
                        :, st, 8 * cv2:8 * cv2 + 8, 0:64]
                    vcp[k] = emit("dve", mk(nc.vector.tensor_copy, dstv,
                                            ps.rearrange("p (h c) -> p h c", c=64)),
                                  waits=[(cP, pm), (cV, vones)], inc=cV)
            v_done = cV.n

            trigA = V(TRIG, 128, 2048, BF16)
            trigB = V(TRIG + 2048, 128, 2048, BF16)
            mz = V(MZ, 128, 2048, BF16)
            mz2 = V(MZ2, 128, 2048, BF16)
            rotq = V(ROTQ, 64, 2048, BF16)
            rotk = V(ROTK, 64, 2048, BF16)
            r_f32 = V(RF32, 1, 2048, F32)
            r_bf = V(RBF, 1, 2048, BF16)
            rb = V(RB_, 128, 2048, BF16)
            vbias = emit("dve", mk(nc.vector.memset, bias_cols, 0.0), inc=cV)
            head_end_v = 0
            exp_last = 0
            vo = 0
            vrbc = 0
            for h in range(NH):
                tv = 0
                for dst_, s_, r0 in ((trigA, "cosT", 0), (trigA, "sinT", 32),
                                     (trigA, "cosT", 64), (trigA, "sinT", 96),
                                     (trigB, "sinT", 0), (trigB, "cosT", 32),
                                     (trigB, "sinT", 64), (trigB, "cosT", 96)):
                    tv = emit("gp", mk(nc.gpsimd.dma_start, out=dst_[r0:r0 + 32, :],
                                       in_=d[s_][32 * h:32 * h + 32, :]),
                              waits=[(cV, head_end_v)], inc=cW)
                blkh = qk[:, 2048 * h:2048 * (h + 1)]
                rot_v = 0
                groups = (
                    (rotq[0:32, :], 0, 0, 32, SUB),
                    (rotq[32:64, :], 32, 0, 32, ADD),
                    (rotk[0:32, :], 0, 64, 96, SUB),
                    (rotk[32:64, :], 32, 64, 96, ADD),
                )
                first = True
                for dst_s, tb_, be, bo, op in groups:
                    tg = trigA if op == SUB else trigB
                    w = [(cW, tv), (cV, qk_done)] if first else ()
                    first = False
                    p1 = mz[tb_:tb_ + 32, :]
                    p2 = mz2[tb_:tb_ + 32, :]
                    emit("dve", mk(nc.vector.tensor_tensor, p1,
                                   blkh[be:be + 32, :], tg[be:be + 32, :], MUL),
                         waits=w, inc=cV)
                    emit("dve", mk(nc.vector.tensor_tensor, p2,
                                   blkh[bo:bo + 32, :], tg[bo:bo + 32, :], MUL),
                         inc=cV)
                    rot_v = emit("dve", mk(nc.vector.tensor_tensor, dst_s, p1,
                                           p2, op), inc=cV)
                prev_exp, prev_mask, av_last = 0, 0, 0
                for si in range(NST):
                    a0 = 128 * si
                    shift = 1024 if (si >= 8 and si % 2 == 0) else 0
                    ps0 = a0 - shift
                    pm = 0
                    for (c0, n) in chunks512(ps0, T - shift):
                        w = ()
                        if c0 == ps0:
                            w = [(cV, rot_v), (cA, prev_exp), (cV, prev_mask),
                                 (cV, vrbc)]
                        pm = emit("pe", mk(nc.tensor.matmul, psA[:, c0:c0 + n],
                                           rotk[:, a0:a0 + 128],
                                           rotq[:, c0 + shift:c0 + shift + n],
                                           start=True, stop=True), waits=w, inc=cP)
                    prev_mask = emit("dve", mk(nc.vector.tensor_tensor,
                                               psA[:, ps0:ps0 + 128],
                                               psA[:, ps0:ps0 + 128], trimask, ADD),
                                     waits=[(cP, pm), (cX, ld_last)], inc=cV)
                    rng = T - a0
                    pv = p_sb[:, P_OFF[si]:P_OFF[si] + rng]
                    ea = emit("act", mk(nc.scalar.activation, pv,
                                        psA[:, ps0:ps0 + rng], AF.Exp,
                                        bias=bias_cols[:, si:si + 1], scale=0.125),
                              waits=[(cV, prev_mask), (cV, vbias)], inc=cA)
                    prev_exp = exp_last = ea
                    for (c0, n) in chunks512(a0, T):
                        bank = c0 // 512
                        w = ()
                        if c0 == a0:
                            w = [(cA, ea), (cV, v_done), (cV, vo), (cV, vrbc)]
                        av_last = emit("pe", mk(nc.tensor.matmul,
                            psB[0:65, c0:c0 + n],
                            vsb[:, 1040 * si + 65 * h:1040 * si + 65 * h + 65],
                            pv[:, c0 - a0:c0 - a0 + n],
                            start=(si == 0),
                            stop=(si == min(NST - 1, 4 * bank + 3))),
                            waits=w, inc=cP)
                emit("act", mk(nc.scalar.activation, r_f32, psB[64:65, :],
                               AF.Ln), waits=[(cP, av_last)], inc=cA)
                vra = emit("act", mk(nc.scalar.activation, r_bf, r_f32, AF.Exp,
                                     scale=-1.0), inc=cA)
                vrb = 0
                pob = 0
                for cc in range(4):
                    pob = emit("pe", mk(nc.tensor.matmul,
                                        psA[:, 512 * cc:512 * cc + 512], ones_r128b,
                                        r_bf[0:1, 512 * cc:512 * cc + 512],
                                        start=True, stop=True),
                               waits=([(cA, vra), (cA, exp_last), (cV, prev_mask)]
                                      if cc == 0 else ()),
                               inc=cP if cc == 3 else None)
                vrbc = emit("dve", mk(nc.vector.tensor_copy, rb, psA[:, :]),
                            waits=[(cP, pob)], inc=cV)
                hp, pr = h // 2, 64 * (h % 2)
                vo = emit("dve", mk(nc.vector.tensor_tensor,
                                    o_all[pr:pr + 64, 2048 * hp:2048 * hp + 2048],
                                    psB[0:64, :], rb[0:64, :], MUL), inc=cV)
                if h < NH - 1:
                    for si in range(NST):
                        rng = T - 128 * si
                        emit("dve", mk(nc.vector.tensor_tensor, mz[:, 0:rng],
                                       p_sb[:, P_OFF[si]:P_OFF[si] + rng],
                                       rb[:, 128 * si:T], MUL), inc=cV)
                        emit("dve", mk(nc.vector.tensor_reduce,
                                       casc_raw[:, si:si + 1], mz[:, 0:rng],
                                       mybir.AxisListType.X, ADD), inc=cV)
                    emit("dve", mk(nc.vector.drain))
                    vbias = emit("dve", mk(nc.vector.tensor_scalar, bias_cols,
                                           casc_raw, cwb[:, h:h + 1], 1.0 / T,
                                           MUL, MUL),
                                 waits=[(cX, ld_last)], inc=cV)
                head_end_v = cV.n

            barrier()
            wproj_sb = V(WPROJ, 128, 8192, BF16)
            axold = V(AXOLD, 128, 4096, F32)
            axnew = V(AXNEW, 128, 4096, F32)
            wpld = emit("gp", mk(nc.gpsimd.dma_start,
                                 out=wproj_sb.rearrange("p (j c) -> p j c", c=1024),
                                 in_=d["wproj"].rearrange("(j p) c -> p j c", p=128)),
                        inc=cW)
            src_r = src.rearrange("(j p) s -> p j s", p=128)
            dst_r = dst.rearrange("(j p) s -> p j s", p=128)
            tt_vals = {}
            xnew_dma, last_tt = 0, 0
            for sc in range(NSC):
                s0 = 512 * sc
                xld = emit("sp", mk(nc.sync.dma_start,
                                    out=axold.rearrange("p (j s) -> p j s", s=512),
                                    in_=src_r[:, :, s0:s0 + 512]),
                           waits=[(cV, last_tt)], inc=cX)
                for ht in range(HJ):
                    k = sc * HJ + ht
                    ps = psA[:, 512 * (k % 2):512 * (k % 2) + 512]
                    pm = 0
                    for cj in range(HJ):
                        w = ()
                        if cj == 0:
                            w = [(cW, wpld), (cV, vo), (cV, tt_vals.get(k - 2, 0))]
                        pm = emit("pe", mk(nc.tensor.matmul, ps,
                            wproj_sb[:, 1024 * cj + 128 * ht:1024 * cj + 128 * ht + 128],
                            o_all[:, 2048 * cj + 512 * sc:2048 * cj + 512 * sc + 512],
                            start=(cj == 0), stop=(cj == HJ - 1)),
                            waits=w, inc=cP if cj == HJ - 1 else None)
                    tt_vals[k] = emit("dve", mk(nc.vector.tensor_tensor,
                        axnew[:, 512 * ht:512 * ht + 512], ps,
                        axold[:, 512 * ht:512 * ht + 512], ADD),
                        waits=[(cP, pm), (cX, xld), (cX, xnew_dma)], inc=cV)
                last_tt = tt_vals[sc * HJ + HJ - 1]
                xnew_dma = emit("sp", mk(nc.sync.dma_start,
                                         out=dst_r[:, :, s0:s0 + 512],
                                         in_=axnew.rearrange("p (j s) -> p j s", s=512)),
                                waits=[(cV, last_tt)], inc=cX)
            return xnew_dma

        emit_ffn(d["xT"], xres)
        emit_attn(xres, xres)
        emit_ffn(xres, outT)
        emit("sp", mk(nc.sync.nop), waits=[(cX, cX.n)])

        blk.tensor(lambda e: [f(e) for f in Q["pe"]] and None)
        blk.scalar(lambda e: [f(e) for f in Q["act"]] and None)
        blk.vector(lambda e: [f(e) for f in Q["dve"]] and None)
        blk.sync(lambda e: [f(e) for f in Q["sp"]] and None)
        blk.gpsimd(lambda e: [f(e) for f in Q["gp"]] and None)
    return nc


def _rope_tables():
    import jax
    import jax.numpy as jnp
    with jax.default_device(jax.devices("cpu")[0]):
        inv = 1.0 / 10000.0 ** (jnp.arange(0, HD, 2, dtype=jnp.float32) / HD)
        t = jnp.arange(T, dtype=jnp.float32)
        cos = np.empty((NH, 32, T), np.float32)
        sin = np.empty((NH, 32, T), np.float32)
        for i in range(NH):
            fr = jnp.outer(t, inv * (10000.0 ** (i / NH)))
            cos[i] = np.asarray(jnp.cos(fr)).T
            sin[i] = np.asarray(jnp.sin(fr)).T
    return cos.reshape(NH * 32, T), sin.reshape(NH * 32, T)


def kernel(L0_attn_x, L0_attn_Wqkv, L0_attn_Wproj, L0_attn_cascade_w,
           L0_ffn_gate_w1, L0_ffn_gate_b1, L0_ffn_gate_w2, L0_ffn_gate_b2,
           L0_ffn_W1, L0_ffn_b1, L0_ffn_W2, L0_ffn_b2,
           L0_ffn_TW1, L0_ffn_Tb1, L0_ffn_TW2, L0_ffn_Tb2):
    f32 = np.float32
    x = np.asarray(L0_attn_x, f32)
    perm = np.empty(3 * H, np.int64)
    for i in range(NH):
        base = HD * i
        perm[base:base + 32] = base + np.arange(0, HD, 2)
        perm[base + 32:base + 64] = base + np.arange(1, HD, 2)
    perm[H:2 * H] = perm[:H] + H
    perm[2 * H:] = np.arange(2 * H, 3 * H)
    wqkv = np.ascontiguousarray(np.asarray(L0_attn_Wqkv, f32)[:, perm])
    cos, sin = _rope_tables()
    tri = np.where(np.arange(128)[:, None] > np.arange(128)[None, :],
                   np.float32(-1e30), np.float32(0.0)).astype(f32)
    common = {
        "wqkv": wqkv,
        "wproj": np.ascontiguousarray(np.asarray(L0_attn_Wproj, f32)),
        "gw1t": np.ascontiguousarray(np.asarray(L0_ffn_gate_w1, f32).T),
        "gw2t": np.ascontiguousarray(np.asarray(L0_ffn_gate_w2, f32).T),
        "gb1c": np.ascontiguousarray(np.asarray(L0_ffn_gate_b1, f32).reshape(8, 128).T),
        "gb2r": np.ascontiguousarray(np.asarray(L0_ffn_gate_b2, f32)[None, :]),
        "gb2c": np.ascontiguousarray(np.asarray(L0_ffn_gate_b2, f32)[:, None]),
        "w1t": np.ascontiguousarray(np.asarray(L0_ffn_W1, f32).T),
        "w2t": np.ascontiguousarray(np.asarray(L0_ffn_W2, f32).T),
        "tw1t": np.ascontiguousarray(np.asarray(L0_ffn_TW1, f32).transpose(0, 2, 1)),
        "tw2t": np.ascontiguousarray(np.asarray(L0_ffn_TW2, f32).transpose(0, 2, 1)),
        "b1c": np.ascontiguousarray(np.asarray(L0_ffn_b1, f32).reshape(32, 128).T),
        "b2r": np.ascontiguousarray(np.asarray(L0_ffn_b2, f32)[None, :]),
        "tb1": np.ascontiguousarray(np.asarray(L0_ffn_Tb1, f32)),
        "tb2": np.ascontiguousarray(np.asarray(L0_ffn_Tb2, f32)),
        "cwb": np.ascontiguousarray(np.broadcast_to(
            np.asarray(L0_attn_cascade_w, f32)[None, :], (128, NH - 1))),
        "cosT": cos, "sinT": sin, "trimask": tri,
    }
    in_maps = []
    for b in range(B):
        m = dict(common)
        m["xT"] = np.ascontiguousarray(x[b].T)
        in_maps.append(m)
    nc = build_program()
    res = run_bass_kernel_spmd(nc, in_maps, core_ids=list(range(B)))
    out = np.empty((B, T, H), f32)
    for b in range(B):
        out[b] = res.results[b]["outT"].T
    return out
